# version 04
# baseline (speedup 1.0000x reference)
import contextlib
import numpy as np
import ml_dtypes
import concourse.bass as bass
import concourse.mybir as mybir
from concourse.bass_utils import run_bass_kernel_spmd

F32 = mybir.dt.float32
BF16 = mybir.dt.bfloat16
AF = mybir.ActivationFunctionType
ALU = mybir.AluOpType

P = 2054
T = 2048
HALF = 3
D = 1024
DK = 8
NH = 8
FD = 64
DFF = 4096
MF = 32
MAGIC = 12582912.0
EPS_LN = 1e-5
EPS_Q = 1e-5
EPS_Z = 1e-8
N_ELEM_W = float(DFF * D)
MASKW = 134
N_XT = 17
N_TC = 16

SLABS5 = [(0, 512), (512, 512), (1024, 512), (1536, 512), (2048, 6)]
KCS = [(k * 128, min(128, P - k * 128)) for k in range(17)]


def _bwin(kc):
    j0, w = KCS[kc]
    return max(HALF, j0 - HALF), min(P - HALF, j0 + w + HALF)


SLAB_PARTS = [[] for _ in range(4)]
SLAB_LAST_KC = [0] * 4
for _kc in range(17):
    _tb, _te = _bwin(_kc)
    for _u in range(4):
        _su, _eu = HALF + 512 * _u, HALF + 512 * (_u + 1)
        _lo, _hi = max(_tb, _su), min(_te, _eu)
        if _lo < _hi:
            SLAB_PARTS[_u].append((_kc, _lo - _tb, _hi - _tb, _lo - _su))
            SLAB_LAST_KC[_u] = max(SLAB_LAST_KC[_u], _kc)


def build_nc(cut=None):
    nc = bass.Bass("TRN2", target_bir_lowering=False)

    x_tm_d = nc.dram_tensor("x_tm", [P, D], F32, kind="ExternalInput").ap()
    wqkv_d = nc.dram_tensor("wqkv", [D, 3 * D], F32, kind="ExternalInput").ap()
    proj_d = nc.dram_tensor("proj", [128, FD], F32, kind="ExternalInput").ap()
    wout_d = nc.dram_tensor("wout", [D, D], F32, kind="ExternalInput").ap()
    bout_d = nc.dram_tensor("bout", [1, D], F32, kind="ExternalInput").ap()
    w1t_d = nc.dram_tensor("w1t", [D, DFF], F32, kind="ExternalInput").ap()
    b1t_d = nc.dram_tensor("b1t", [128, MF], F32, kind="ExternalInput").ap()
    w2t_d = nc.dram_tensor("w2t", [DFF, D], F32, kind="ExternalInput").ap()
    b2t_d = nc.dram_tensor("b2t", [128, DK], F32, kind="ExternalInput").ap()
    ident_d = nc.dram_tensor("ident", [128, 128], BF16, kind="ExternalInput").ap()
    masks_d = nc.dram_tensor("masks", [128, 17 * MASKW], BF16, kind="ExternalInput").ap()
    emask_d = nc.dram_tensor("emask", [64, 6], F32, kind="ExternalInput").ap()
    out_d = nc.dram_tensor("out", [T, D], F32, kind="ExternalOutput").ap()

    wq1_sc = nc.dram_tensor("wq1_sc", [16, 128, 8, 256], BF16, kind="Internal").ap()
    wq2_sc = nc.dram_tensor("wq2_sc", [8, 128, 32, 128], BF16, kind="Internal").ap()
    zdram = nc.dram_tensor("zdram", [NH, T], BF16, kind="Internal").ap()
    zdram_f = nc.dram_tensor("zdram_f", [NH, T], F32, kind="Internal").ap()

    steps = []
    PH = {}
    C = {"SQ": 0, "TE": 0, "VE": 0, "SE": 0, "GE": 0}

    def bump(name, n=1):
        C[name] += n
        return C[name]

    es = contextlib.ExitStack()
    with es:
        sq = es.enter_context(nc.semaphore("sq"))
        te_s = es.enter_context(nc.semaphore("te"))
        ve_s = es.enter_context(nc.semaphore("ve"))
        se_s = es.enter_context(nc.semaphore("se"))
        ge_s = es.enter_context(nc.semaphore("ge"))
        SEM = {"SQ": sq, "TE": te_s, "VE": ve_s, "SE": se_s, "GE": ge_s}

        sem_es = es

        def dch(name):
            if name not in SEM:
                SEM[name] = sem_es.enter_context(nc.semaphore(name))
                C[name] = 0
            return SEM[name]

        def dma(ch, out, in_, waits=(), q="sync"):
            h = dch(ch)
            w = [(s, v) for (s, v) in waits if v]
            val = bump(ch, 16)

            def f(e, out=out, in_=in_, w=w, h=h):
                for (s, v) in w:
                    e.wait_ge(SEM[s], v)
                e.dma_start(out=out, in_=in_).then_inc(h, 16)
            import traceback
            steps.append((q, f, traceback.extract_stack()[-2].lineno))
            return val

        def op(eng, sem_name, emitf, waits=(), inc=1):
            w = [(s, v) for (s, v) in waits if v]
            val = bump(sem_name, inc) if inc else C[sem_name]

            def f(e, emitf=emitf, w=w, inc=inc, sem_name=sem_name):
                for (s, v) in w:
                    e.wait_ge(SEM[s], v)
                ins = emitf(e)
                if inc:
                    ins.then_inc(SEM[sem_name], inc)
            import traceback
            steps.append((eng, f, traceback.extract_stack()[-2].lineno))
            return val

        ident = es.enter_context(nc.sbuf_tensor("sb_ident", [128, 128], BF16))
        emask = es.enter_context(nc.sbuf_tensor("sb_emask", [64, 6], F32))
        proj_f = es.enter_context(nc.sbuf_tensor("proj_f", [128, FD], F32))
        proj_b = es.enter_context(nc.sbuf_tensor("proj_b", [128, FD], BF16))
        b1t = es.enter_context(nc.sbuf_tensor("sb_b1t", [128, MF], F32))
        b2t = es.enter_context(nc.sbuf_tensor("sb_b2t", [128, DK], F32))
        bout_f = es.enter_context(nc.sbuf_tensor("bout_f", [1, D], F32))
        bout_b = es.enter_context(nc.sbuf_tensor("bout_b", [1, D], BF16))
        ones_rb = es.enter_context(nc.sbuf_tensor("ones_rb", [1, 128], BF16))
        ones_mb = es.enter_context(nc.sbuf_tensor("ones_mb", [128, 128], BF16))
        ones_cf = es.enter_context(nc.sbuf_tensor("ones_cf", [128, 1], F32))
        magic_c = es.enter_context(nc.sbuf_tensor("magic_c", [128, 1], F32))
        ones_rf = es.enter_context(nc.sbuf_tensor("ones_rf", [1, 128], F32))

        xn = es.enter_context(nc.sbuf_tensor("xn", [128, DK, P], BF16))
        attn_r = es.enter_context(nc.sbuf_tensor("attn_r", [128, NH, T], BF16))

        v_c0 = 0
        for (dst, src) in [(ident[:], ident_d), (emask[:], emask_d),
                           (proj_f[:], proj_d), (b1t[:], b1t_d), (b2t[:], b2t_d),
                           (bout_f[:], bout_d)]:
            v_c0 = dma("cA", dst, src)
        op("vector", "VE", lambda e: e.memset(ones_rb[:], 1.0), inc=0)
        op("vector", "VE", lambda e: e.memset(ones_mb[:], 1.0), inc=0)
        op("vector", "VE", lambda e: e.memset(ones_cf[:], 1.0), inc=0)
        op("vector", "VE", lambda e: e.memset(magic_c[:], MAGIC), inc=0)
        op("vector", "VE", lambda e: e.memset(ones_rf[:], 1.0), inc=0)
        op("vector", "VE", lambda e: e.tensor_copy(proj_b[:], proj_f[:]),
           waits=[("cA", v_c0)], inc=0)
        v_consts = op("vector", "VE", lambda e: e.tensor_copy(bout_b[:], bout_f[:]),
                      waits=[("cA", v_c0)])

        v_vevac = {}
        v_aevac = {}
        v_zevac = {}
        v_div = {}

        with contextlib.ExitStack() as es_vh:
            v_tm = es_vh.enter_context(nc.sbuf_tensor("v_tm", [128, N_XT, D], BF16))

            tr_done = {}
            v_xn_evac = {}
            with contextlib.ExitStack() as es_lnv:
                ptr = [es_lnv.enter_context(nc.psum_tensor(f"ptr{i}", [128, D], BF16))
                       for i in range(2)]
                pv = [es_lnv.enter_context(nc.psum_tensor(f"pv{i}", [128, D], F32))
                      for i in range(2)]

                with contextlib.ExitStack() as es_ln:
                    xtm = es_ln.enter_context(nc.sbuf_tensor("xtm", [128, N_XT, D], F32))
                    junk = es_ln.enter_context(nc.sbuf_tensor("junk", [128, D], F32))
                    ssum = es_ln.enter_context(nc.sbuf_tensor("ssum", [128, N_XT], F32))
                    ssq = es_ln.enter_context(nc.sbuf_tensor("ssq", [128, N_XT], F32))
                    m2c = es_ln.enter_context(nc.sbuf_tensor("m2c", [128, N_XT], F32))
                    rstd = es_ln.enter_context(nc.sbuf_tensor("rstd", [128, N_XT], F32))
                    nmr = es_ln.enter_context(nc.sbuf_tensor("nmr", [128, N_XT], F32))
                    xnt = es_ln.enter_context(nc.sbuf_tensor("xnt", [128, 2, D], BF16))

                    dma("cX", xtm[:, 0:16, :],
                        bass.AP(x_tm_d.tensor, 0, [[D, 128], [128 * D, 16], [1, D]]))
                    v_x1 = dma("cX", xtm[0:6, 16, :], x_tm_d[2048:2054, :])

                    op("vector", "VE", lambda e: e.memset(ssum[:], 0.0), inc=0)
                    v_ms = op("vector", "VE", lambda e: e.memset(ssq[:], 0.0))
                    s_stats = None
                    for tc in range(N_XT):
                        rows = 128 if tc < 16 else 6
                        wv = [("cX", v_x1), ("VE", v_ms)] if tc == 0 else [("SE", s_stats)]
                        s_prev = op("scalar", "SE", lambda e, tc=tc, rows=rows: e.activation(
                            junk[0:rows, :], xtm[0:rows, tc, :], AF.Square,
                            accum_out=ssq[0:rows, tc:tc + 1]), waits=wv)
                        s_stats = op("scalar", "SE", lambda e, tc=tc, rows=rows: e.activation(
                            junk[0:rows, :], xtm[0:rows, tc, :], AF.Identity,
                            accum_out=ssum[0:rows, tc:tc + 1]),
                            waits=[("SE", s_prev)])

                    w1_ = op("vector", "VE", lambda e: e.tensor_scalar(
                        ssum[:], ssum[:], 1.0 / D, None, ALU.mult),
                        waits=[("SE", s_stats)])
                    w2_ = op("vector", "VE", lambda e: e.tensor_tensor(
                        m2c[:], ssum[:], ssum[:], ALU.mult), waits=[("VE", w1_)])
                    w3_ = op("vector", "VE", lambda e: e.scalar_tensor_tensor(
                        ssq[:], ssq[:], 1.0 / D, m2c[:], ALU.mult, ALU.subtract),
                        waits=[("VE", w2_)])
                    w4_ = op("vector", "VE", lambda e: e.tensor_scalar(
                        ssq[:], ssq[:], EPS_LN, None, ALU.add), waits=[("VE", w3_)])
                    v_stats = op("vector", "VE", lambda e: e.reciprocal(m2c[:], ssq[:]),
                                 waits=[("VE", w4_)])
                    s_rstd = op("scalar", "SE", lambda e: e.activation(
                        rstd[:], m2c[:], AF.Sqrt), waits=[("VE", v_stats)])
                    v_nmr = op("vector", "VE", lambda e: e.scalar_tensor_tensor(
                        nmr[:], ssum[:], -1.0, rstd[:], ALU.mult, ALU.mult),
                        waits=[("SE", s_rstd), ("VE", v_stats)])

                    for tc in range(N_XT):
                        rows = 128 if tc < 16 else 6
                        buf = tc % 2
                        s_w = [("VE", v_nmr)]
                        if tc >= 2:
                            s_w.append(("TE", tr_done[tc - 2]))
                        s_xn = op("scalar", "SE",
                            lambda e, tc=tc, rows=rows, buf=buf: e.activation(
                                xnt[0:rows, buf, :], xtm[0:rows, tc, :], AF.Identity,
                                scale=rstd[0:rows, tc:tc + 1],
                                bias=nmr[0:rows, tc:tc + 1]),
                            waits=s_w)
                        for j in range(DK):
                            w_list = []
                            if j == 0:
                                w_list = [("SE", s_xn)]
                                if tc == 0:
                                    w_list.append(("cA", v_c0))
                                if tc >= 2:
                                    w_list.append(("VE", v_xn_evac[tc - 2]))
                            tr_done[tc] = op("tensor", "TE",
                                lambda e, rows=rows, buf=buf, j=j: e.transpose(
                                    ptr[buf][:, 128 * j:128 * j + rows],
                                    xnt[0:rows, buf, 128 * j:128 * (j + 1)],
                                    ident[0:rows, 0:rows]),
                                waits=w_list)
                        v_xn_evac[tc] = op("vector", "VE",
                            lambda e, tc=tc, rows=rows, buf=buf: e.tensor_copy(
                                xn[:, :, 128 * tc:128 * tc + rows],
                                ptr[buf][:].rearrange("p (j r) -> p j r", j=DK)[:, :, 0:rows]),
                            waits=[("TE", tr_done[tc])])

                PH["ln"] = len(steps)
                ln_end = dict(C)
                with contextlib.ExitStack() as es_v:
                    wvf = [es_v.enter_context(nc.sbuf_tensor(f"wvf{i}", [128, D], F32))
                           for i in range(2)]
                    wv_b = es_v.enter_context(nc.sbuf_tensor("wv_b", [128, DK, D], BF16))

                    v_wvc = {}
                    for kc in range(DK):
                        ch = f"cWV{kc % 2}"
                        xw = [("VE", v_wvc.get(kc - 2))]
                        if kc < 2:
                            xw += [("SE", ln_end["SE"]), ("TE", ln_end["TE"]),
                                   ("VE", ln_end["VE"])]
                        d_wv = dma(ch, wvf[kc % 2][:],
                                   wqkv_d[128 * kc:128 * kc + 128, 2048:3072],
                                   waits=xw)
                        v_wvc[kc] = op("vector", "VE", lambda e, kc=kc: e.tensor_copy(
                            wv_b[:, kc, :], wvf[kc % 2][:]), waits=[(ch, d_wv)])

                    for tc in range(N_XT):
                        rows = 128 if tc < 16 else 6
                        buf = tc % 2
                        t_last = None
                        for kc in range(DK):
                            for hh in range(2):
                                wl = []
                                if kc == 0 and hh == 0:
                                    wl = [("VE", v_xn_evac[tc]), ("VE", v_wvc[DK - 1])]
                                    if tc >= 2:
                                        wl.append(("VE", v_vevac[tc - 2]))
                                t_last = op("tensor", "TE",
                                    lambda e, tc=tc, rows=rows, buf=buf, kc=kc, hh=hh:
                                    e.matmul(
                                        pv[buf][0:rows, 512 * hh:512 * hh + 512],
                                        xn[:, kc, 128 * tc:128 * tc + rows],
                                        wv_b[:, kc, 512 * hh:512 * hh + 512],
                                        start=(kc == 0), stop=(kc == DK - 1),
                                        skip_group_check=True),
                                    waits=wl)
                        v_vevac[tc] = op("vector", "VE",
                            lambda e, tc=tc, rows=rows, buf=buf: e.tensor_copy(
                                v_tm[0:rows, tc, :], pv[buf][0:rows, :]),
                            waits=[("TE", t_last)])

            PH["v"] = len(steps)
            v_end = dict(C)
            with contextlib.ExitStack() as es_h:
                masks = es_h.enter_context(
                    nc.sbuf_tensor("sb_masks", [128, 17 * MASKW], BF16))
                whf = [es_h.enter_context(nc.sbuf_tensor(f"whf{i}", [128, D], F32))
                       for i in range(2)]
                whb = [es_h.enter_context(nc.sbuf_tensor(f"whb{i}", [128, D], BF16))
                       for i in range(2)]
                qfm = [es_h.enter_context(nc.sbuf_tensor(f"qfm{i}", [128, 512], BF16))
                       for i in range(2)]
                e_scr = [es_h.enter_context(nc.sbuf_tensor(f"e_scr{i}", [64, 512], F32))
                         for i in range(2)]
                r_scr = [es_h.enter_context(nc.sbuf_tensor(f"r_scr{i}", [64, 512], F32))
                         for i in range(2)]
                q_phi = es_h.enter_context(nc.sbuf_tensor("q_phi", [64, P], BF16))
                k_phi = es_h.enter_context(nc.sbuf_tensor("k_phi", [64, P], BF16))
                s_bf = [es_h.enter_context(nc.sbuf_tensor(f"s_bf{i}", [128, MASKW], BF16))
                        for i in range(4)]
                z_all = es_h.enter_context(nc.sbuf_tensor("z_all", [NH, T], F32))
                z2 = es_h.enter_context(nc.sbuf_tensor("z2", [NH, T], F32))
                zinv_b = es_h.enter_context(nc.sbuf_tensor("zinv_b", [NH, T], BF16))
                zrow_b = es_h.enter_context(nc.sbuf_tensor("zrow_b", [1, T], BF16))
                z_stage = [es_h.enter_context(nc.sbuf_tensor(f"z_stage{i}", [1, 512], F32))
                           for i in range(2)]
                qbuf = [es_h.enter_context(nc.sbuf_tensor(f"qb{i}", [128, 2048], F32))
                        for i in range(3)]
                qo = [es_h.enter_context(nc.sbuf_tensor(f"qo{i}", [128, 2048], BF16))
                      for i in range(2)]
                gam = es_h.enter_context(nc.sbuf_tensor("gam", [128, 32], F32))
                gc2 = es_h.enter_context(nc.sbuf_tensor("gc2", [128, 2], F32))
                grow = es_h.enter_context(nc.sbuf_tensor("grow", [1, 2], F32))
                gtmp = es_h.enter_context(nc.sbuf_tensor("gtmp", [128, 2], F32))
                invg = es_h.enter_context(nc.sbuf_tensor("invg", [128, 2], F32))
                pq = [es_h.enter_context(nc.psum_tensor(f"pq{i}", [128, 512], F32))
                      for i in range(2)]
                pS = [es_h.enter_context(nc.psum_tensor(f"pS{i}", [128, 512], F32))
                      for i in range(2)]
                pA = [es_h.enter_context(nc.psum_tensor(f"pA{i}", [128, 512], F32))
                      for i in range(2)]
                pZ = [es_h.enter_context(nc.psum_tensor(f"pZ{i}", [128, 512], F32))
                      for i in range(2)]

                d_masks = dma("cM", masks[:], masks_d,
                              waits=[("TE", v_end["TE"]), ("VE", v_end["VE"])])

                def qpiece_in(i):
                    if i < 16:
                        kc, hh = i // 2, i % 2
                        return w1t_d[128 * kc:128 * kc + 128,
                                     2048 * hh:2048 * hh + 2048]
                    j = i - 16
                    return bass.AP(w2t_d.tensor, 256 * j * D,
                                   [[D, 128], [128 * D, 2], [1, D]])

                def qpiece_out(i, buf):
                    src = qo[buf][:]
                    if i < 16:
                        kc, hh = i // 2, i % 2
                        o = bass.AP(wq1_sc.tensor,
                                    (8 * hh) * (128 * 8 * 256) + kc * 256,
                                    [[8 * 256, 128], [128 * 8 * 256, 8], [1, 256]])
                        s = src.rearrange("p (mp c) -> p mp c", mp=8)
                        return [(o, s)]
                    j = i - 16
                    pairs = []
                    for rr in range(2):
                        o = bass.AP(wq2_sc.tensor, (2 * j + rr) * 128,
                                    [[32 * 128, 128], [128 * 32 * 128, 8], [1, 128]])
                        s = bass.AP(qo[buf], rr * 1024,
                                    [[2048, 128], [128, 8], [1, 128]])
                        pairs.append((o, s))
                    return pairs

                q_dma = {}
                q_abs = {}
                q_p1 = {}
                q_p2 = {}
                q_p3 = {}
                q_wr = {}

                def emit_gamma_piece(i):
                    b = i % 3
                    xw = [("SE", q_abs.get(i - 3))]
                    if i < 3:
                        xw += [("TE", v_end["TE"]), ("VE", v_end["VE"])]
                    q_dma[i] = dma(f"cQB{b}", qbuf[b][:], qpiece_in(i), waits=xw, q="gpsimd")
                    q_abs[i] = op("scalar", "SE", lambda e, b=b, i=i: e.activation(
                        qo[i % 2][:], qbuf[b][:], AF.Abs,
                        accum_out=gam[:, i:i + 1]),
                        waits=[(f"cQB{b}", q_dma[i]), ("SE", q_abs.get(i - 2))])

                def emit_invg():
                    op("vector", "VE", lambda e: e.tensor_reduce(
                        gc2[:, 0:1], gam[:, 0:16], mybir.AxisListType.X, ALU.add),
                        waits=[("SE", q_abs[31])], inc=0)
                    v_gc = op("vector", "VE", lambda e: e.tensor_reduce(
                        gc2[:, 1:2], gam[:, 16:32], mybir.AxisListType.X, ALU.add))
                    t1 = op("tensor", "TE", lambda e: e.matmul(
                        pq[0][0:1, 0:2], ones_cf[:], gc2[:], start=True, stop=True,
                        skip_group_check=True), waits=[("VE", v_gc)])
                    v_gr = op("vector", "VE", lambda e: e.tensor_copy(
                        grow[:], pq[0][0:1, 0:2]), waits=[("TE", t1)])
                    t2 = op("tensor", "TE", lambda e: e.matmul(
                        pq[0][0:128, 2:4], ones_rf[:], grow[:], start=True, stop=True,
                        skip_group_check=True), waits=[("VE", v_gr)])
                    vg1 = op("vector", "VE", lambda e: e.tensor_scalar(
                        gtmp[:], pq[0][0:128, 2:4], 1.0 / N_ELEM_W, EPS_Q,
                        ALU.mult, ALU.add),
                        waits=[("TE", t2)])
                    return op("vector", "VE", lambda e: e.reciprocal(invg[:], gtmp[:]),
                              waits=[("VE", vg1)])

                def emit_quant_piece(i, v_invg):
                    b = i % 3
                    bo = i % 2
                    xw = [("GE", q_p2.get(i - 3))]
                    if i < 3:
                        xw.append(("SE", q_abs[{0: 30, 1: 31, 2: 29}[i]]))
                    q_dma[100 + i] = dma(f"cQB{b}", qbuf[b][:], qpiece_in(i),
                                         waits=xw, q="gpsimd")
                    col = 0 if i < 16 else 1
                    q_p1[i] = op("vector", "VE", lambda e, b=b, col=col: e.tensor_scalar(
                        qbuf[b][:], qbuf[b][:], invg[:, col:col + 1], MAGIC,
                        ALU.mult, ALU.add),
                        waits=[(f"cQB{b}", q_dma[100 + i]), ("VE", v_invg)])
                    q_p2[i] = op("gpsimd", "GE", lambda e, b=b: e.tensor_scalar(
                        qbuf[b][:], qbuf[b][:], MAGIC, 1.0, ALU.subtract, ALU.min),
                        waits=[("VE", q_p1[i])])
                    q_p3[i] = op("vector", "VE", lambda e, b=b, bo=bo: e.tensor_scalar(
                        qo[bo][:], qbuf[b][:], -1.0, None, ALU.max),
                        waits=[(f"cQW{bo}", q_wr.get(i - 2)), ("GE", q_p2[i])])
                    for (o, s) in qpiece_out(i, bo):
                        q_wr[i] = dma(f"cQW{bo}", o, s, waits=[("VE", q_p3[i])], q="gpsimd")

                d_zs = {}
                v_wcast = {}
                v_fmevac = {}
                v_phislab = {}
                s_exp = {}
                s_relu = {}
                v_sbf = {}
                v_phidone = {}
                t_sbf_free = {}
                v_invg_val = None

                for h in range(NH):
                    if h == 4:
                        v_invg_val = emit_invg()
                    pend = list(range(8 * (h % 4), 8 * (h % 4) + 8))

                    def emit_piece():
                        if not pend:
                            return
                        i = pend.pop(0)
                        if h < 4:
                            emit_gamma_piece(i)
                        else:
                            emit_quant_piece(i, v_invg_val)

                    for side in range(2):
                        coff = side * D + 128 * h
                        ch = f"cWH{side}"
                        xw = [("VE", v_wcast.get((h - 1, side)))]
                        if h == 0:
                            xw += [("TE", v_end["TE"]), ("VE", v_end["VE"])]
                        d_w = dma(ch, whf[side][:].rearrange("p (k c) -> p k c", k=8),
                                  bass.AP(wqkv_d.tensor, coff,
                                          [[3 * D, 128], [128 * 3 * D, 8], [1, 128]]),
                                  waits=xw)
                        v_wcast[(h, side)] = op("vector", "VE",
                            lambda e, side=side: e.tensor_copy(
                                whb[side][:], whf[side][:]),
                            waits=[(ch, d_w)])

                    for side, phi in enumerate([q_phi, k_phi]):
                        for si, (sb0, sw) in enumerate(SLABS5):
                            buf = si % 2
                            t_last = None
                            for kc in range(DK):
                                wl = []
                                if kc == 0:
                                    wl = [("VE", v_wcast[(h, side)]),
                                          ("VE", v_fmevac.get((h, side, si - 2)))]
                                    if side == 0 and si < 2:
                                        wl.append(("VE", v_sbf.get((h - 1, 14 + si))))
                                    if h == 4 and side == 0 and si == 0:
                                        wl.append(("VE", v_invg_val))
                                t_last = op("tensor", "TE",
                                    lambda e, buf=buf, kc=kc, sb0=sb0, sw=sw,
                                           side=side: e.matmul(
                                        pq[buf][:, 0:sw],
                                        whb[side][:, 128 * kc:128 * kc + 128],
                                        xn[:, kc, sb0:sb0 + sw],
                                        start=(kc == 0), stop=(kc == DK - 1),
                                        skip_group_check=True),
                                    waits=wl)
                            v_fmevac[(h, side, si)] = op("vector", "VE",
                                lambda e, buf=buf, sw=sw: e.tensor_copy(
                                    qfm[buf][:, 0:sw], pq[buf][:, 0:sw]),
                                waits=[("TE", t_last)])
                            wl = [("VE", v_fmevac[(h, side, si)]),
                                  ("VE", v_phislab.get((h, side, si - 2))),
                                  ("SE", s_relu.get((h, side, si - 2)))]
                            t_phi = op("tensor", "TE",
                                lambda e, buf=buf, sw=sw: e.matmul(
                                    pS[buf][0:64, 0:sw], proj_b[:],
                                    qfm[buf][:, 0:sw], start=True, stop=True,
                                    skip_group_check=True),
                                waits=wl)
                            s_exp[(h, side, si)] = op("scalar", "SE",
                                lambda e, buf=buf, sw=sw: e.activation(
                                    e_scr[buf][:, 0:sw], pS[buf][0:64, 0:sw], AF.Exp),
                                waits=[("TE", t_phi)])
                            s_relu[(h, side, si)] = op("scalar", "SE",
                                lambda e, buf=buf, sw=sw: e.activation(
                                    r_scr[buf][:, 0:sw], pS[buf][0:64, 0:sw], AF.Relu),
                                waits=[("SE", s_exp[(h, side, si)])])
                            v_phislab[(h, side, si)] = op("vector", "VE",
                                lambda e, buf=buf, sb0=sb0, sw=sw, phi=phi:
                                e.scalar_tensor_tensor(
                                    phi[:, sb0:sb0 + sw], e_scr[buf][:, 0:sw], 1.0,
                                    r_scr[buf][:, 0:sw], ALU.min, ALU.add),
                                waits=[("SE", s_relu[(h, side, si)]), ("VE",
                                       v_phislab.get((h, side, si - 2)))])
                            emit_piece()
                        v_phidone[(h, side)] = v_phislab[(h, side, 4)]

                    op("vector", "VE", lambda e: e.tensor_tensor(
                        k_phi[:, 0:3], k_phi[:, 0:3], emask[:, 0:3], ALU.mult),
                        waits=[("VE", v_phidone[(h, 1)])], inc=0)
                    v_kedge = op("vector", "VE", lambda e: e.tensor_tensor(
                        k_phi[:, 2051:2054], k_phi[:, 2051:2054], emask[:, 3:6],
                        ALU.mult))

                    first_a = [True] * 4
                    first_z = [True] * 4
                    watermark = [0] * 4
                    t_band = {}
                    pSp = [pS[0], pS[1], pq[0], pq[1]]
                    for kc in range(17):
                        j0, w = KCS[kc]
                        tb, te = _bwin(kc)
                        tw = te - tb
                        buf = kc % 4
                        wl = [("VE", v_sbf.get((h, kc - 4)))]
                        if kc == 0:
                            wl = [("VE", v_kedge), ("cM", d_masks),
                                  ("SE", s_relu[(h, 1, 4)])]
                        t_s = op("tensor", "TE",
                            lambda e, buf=buf, j0=j0, w=w, tb=tb, tw=tw: e.matmul(
                                pSp[buf][0:w, 0:tw], k_phi[:, j0:j0 + w],
                                q_phi[:, tb:tb + tw],
                                start=True, stop=True, skip_group_check=True),
                            waits=wl)
                        sbf_prev = t_sbf_free.get((h, kc - 4)) if kc >= 4 else \
                            t_sbf_free.get((h - 1, 13 + kc))
                        v_sbf[(h, kc)] = op("vector", "VE",
                            lambda e, buf=buf, w=w, tw=tw, kc=kc: e.tensor_tensor(
                                s_bf[buf][0:w, 0:tw], pSp[buf][0:w, 0:tw],
                                masks[0:w, MASKW * kc:MASKW * kc + tw], ALU.mult),
                            waits=[("TE", t_s), ("TE", sbf_prev)])
                        for u in range(4):
                            for (k2, a, b_, pa) in SLAB_PARTS[u]:
                                if k2 != kc:
                                    continue
                                wl_a = [("VE", v_sbf[(h, kc)])]
                                if h == 0 and first_a[u]:
                                    wl_a.append(("VE", v_vevac[N_XT - 1]))
                                if first_a[u]:
                                    pkey = (h, u - 2) if u >= 2 else (h - 1, u + 2)
                                    wl_a.append(("VE", v_aevac.get(pkey)))
                                    wl_a.append(("VE", v_zevac.get(pkey)))
                                wm = watermark[u]
                                pieces = []
                                if pa < wm:
                                    pieces.append((a, a + min(wm - pa, b_ - a), pa, False))
                                if pa + (b_ - a) > wm:
                                    a2 = a + max(0, wm - pa)
                                    pieces.append((a2, b_, pa + (a2 - a),
                                                   first_a[u]))
                                for (aa, bb, pp, st_flag) in pieces:
                                    op("tensor", "TE",
                                        lambda e, u=u, aa=aa, bb=bb, pp=pp, buf=buf,
                                               w=w, kc=kc, h=h, fa=st_flag: e.matmul(
                                            pA[u % 2][:, pp:pp + (bb - aa)],
                                            v_tm[0:w, kc, 128 * h:128 * h + 128],
                                            s_bf[buf][0:w, aa:bb],
                                            start=fa, stop=True,
                                            skip_group_check=True),
                                        waits=wl_a)
                                    wl_a = []
                                    t_z = op("tensor", "TE",
                                        lambda e, u=u, aa=aa, bb=bb, pp=pp, buf=buf,
                                               w=w, fz=st_flag: e.matmul(
                                            pZ[u % 2][:, pp:pp + (bb - aa)],
                                            ones_mb[0:w, :], s_bf[buf][0:w, aa:bb],
                                            start=fz, stop=True,
                                            skip_group_check=True))
                                watermark[u] = max(wm, pa + (b_ - a))
                                first_a[u] = False
                                first_z[u] = False
                                t_band[u] = t_z
                                t_sbf_free[(h, kc)] = t_z
                        for u in range(4):
                            if SLAB_LAST_KC[u] != kc:
                                continue
                            idx = h * 4 + u
                            v_aevac[(h, u)] = op("vector", "VE",
                                lambda e, u=u, h=h: e.tensor_copy(
                                    attn_r[:, h, 512 * u:512 * u + 512],
                                    pA[u % 2][:, 0:512]),
                                waits=[("TE", t_band[u])])
                            v_zevac[(h, u)] = op("vector", "VE",
                                lambda e, u=u: e.tensor_copy(
                                    z_stage[u % 2][0:1, :], pZ[u % 2][0:1, 0:512]),
                                waits=[(f"cZS{u % 2}", d_zs.get(idx - 2))])
                            d_zs[idx] = dma(f"cZS{u % 2}",
                                            zdram_f[h:h + 1, 512 * u:512 * u + 512],
                                            z_stage[u % 2][0:1, :],
                                            waits=[("VE", v_zevac[(h, u)])])

                d_zall = dma("cZA", z_all[:], zdram_f,
                             waits=[("cZS0", d_zs[NH * 4 - 2]), ("cZS1", d_zs[NH * 4 - 1])])
                vz1 = op("vector", "VE", lambda e: e.tensor_scalar(
                    z_all[:], z_all[:], EPS_Z, None, ALU.add),
                    waits=[("cZA", d_zall)])
                vz2 = op("vector", "VE", lambda e: e.reciprocal(z2[:], z_all[:]),
                         waits=[("VE", vz1)])
                v_zi = op("vector", "VE", lambda e: e.tensor_copy(zinv_b[:], z2[:]),
                          waits=[("VE", vz2)])
                d_z1 = dma("cZI", zdram, zinv_b[:], waits=[("VE", v_zi)])
                for h in range(NH):
                    d_z2 = dma("cZR", zrow_b[0:1, :],
                               bass.AP(zdram.tensor, h * T, [[T, 1], [1, T]]),
                               waits=[("cZI", d_z1),
                                      ("VE", v_div.get(4 * h - 1))])
                    for u in range(4):
                        idx = h * 4 + u
                        buf = idx % 2
                        wl = [("cZR", d_z2)]
                        if idx >= 2:
                            wl.append(("VE", v_div[idx - 2]))
                        t_r = op("tensor", "TE", lambda e, buf=buf, u=u: e.matmul(
                            pS[buf][:, 0:512], ones_rb[:],
                            zrow_b[0:1, 512 * u:512 * u + 512],
                            start=True, stop=True, skip_group_check=True),
                            waits=wl)
                        v_div[idx] = op("vector", "VE",
                            lambda e, buf=buf, h=h, u=u: e.tensor_tensor(
                                attn_r[:, h, 512 * u:512 * u + 512],
                                attn_r[:, h, 512 * u:512 * u + 512],
                                pS[buf][:, 0:512], ALU.mult),
                            waits=[("TE", t_r)])
                v_div_last = v_div[NH * 4 - 1]
                sq_scr0 = C.get("cQW0", 0)
                sq_scr1 = C.get("cQW1", 0)
            h_end = dict(C)
            PH["heads"] = len(steps)

        with contextlib.ExitStack() as es_f:
            h_sb = es_f.enter_context(nc.sbuf_tensor("h_sb", [128, MF, 512], BF16))
            g_fm = es_f.enter_context(nc.sbuf_tensor("g_fm", [128, DK, 512], BF16))
            g_tm = es_f.enter_context(nc.sbuf_tensor("g_tm", [128, N_TC, D], BF16))
            wq1b = [es_f.enter_context(nc.sbuf_tensor(f"wq1b{i}", [128, 2048], BF16))
                    for i in range(2)]
            w2tb = [es_f.enter_context(nc.sbuf_tensor(f"w2tb{i}", [128, 32 * 128], BF16))
                    for i in range(2)]
            wout_b = es_f.enter_context(nc.sbuf_tensor("wout_b", [128, DK, D], BF16))
            xtf = [es_f.enter_context(nc.sbuf_tensor(f"xtf{i}", [128, D], F32))
                   for i in range(2)]
            ob = [es_f.enter_context(nc.sbuf_tensor(f"ob{i}", [128, D], F32))
                  for i in range(2)]

            v_woc = {}
            for dc in range(DK):
                ch = f"cXT{dc % 2}"
                xw = [("VE", v_woc.get(dc - 2))]
                if dc < 2:
                    xw += [("TE", h_end["TE"]), ("VE", h_end["VE"]),
                           ("SE", h_end["SE"]), ("cQW0", sq_scr0),
                           ("cQW1", sq_scr1)]
                d_wo = dma(ch, xtf[dc % 2][:], wout_d[128 * dc:128 * dc + 128, :],
                           waits=xw)
                v_woc[dc] = op("vector", "VE", lambda e, dc=dc: e.tensor_copy(
                    wout_b[:, dc, :], xtf[dc % 2][:]), waits=[(ch, d_wo)])
            v_wo_last = v_woc[DK - 1]

            s_gev = {}
            v_gt = {}
            with contextlib.ExitStack() as es_p1:
                ph = [es_p1.enter_context(nc.psum_tensor(f"ph{i}", [128, 512], F32))
                      for i in range(2)]
                pg = [es_p1.enter_context(nc.psum_tensor(f"pg{i}", [128, 512], F32))
                      for i in range(2)]
                pgt = [es_p1.enter_context(nc.psum_tensor(f"pgt{i}", [128, D], BF16))
                       for i in range(2)]

                h_ev = {}
                h_eng = {}
                t_wq1 = {}
                t_ffn2_last = {}
                for u in range(4):
                    xs = HALF + 512 * u
                    for mp in range(16):
                        wv = []
                        if (u, mp - 2) in t_wq1:
                            wv.append(("TE", t_wq1[(u, mp - 2)]))
                        elif u > 0 and mp < 2:
                            wv.append(("TE", t_wq1[(u - 1, 14 + mp)]))
                        if u == 0 and mp < 2:
                            wv.append(("cQW0", sq_scr0))
                            wv.append(("cQW1", sq_scr1))
                            wv += [("TE", h_end["TE"]), ("VE", h_end["VE"]),
                                   ("SE", h_end["SE"])]
                        d_wq1 = dma(
                            f"cW1{mp % 2}",
                            wq1b[mp % 2][:],
                            bass.AP(wq1_sc.tensor, mp * (128 * 8 * 256),
                                    [[8 * 256, 128], [1, 8 * 256]]),
                            waits=wv, q="gpsimd")
                        for mo in range(2):
                            m = 2 * mp + mo
                            t_last = None
                            for kc in range(DK):
                                wl = []
                                if kc == 0:
                                    wl = [(f"cW1{mp % 2}", d_wq1)]
                                    if u == 0 and m == 0:
                                        wl.append(("VE", v_div_last))
                                    if m >= 2:
                                        wl.append((h_eng[(u, m - 2)], h_ev[(u, m - 2)]))
                                    elif u > 0:
                                        wl.append((h_eng[(u - 1, 30 + m)],
                                                   h_ev[(u - 1, 30 + m)]))
                                t_last = op("tensor", "TE",
                                    lambda e, m=m, mo=mo, kc=kc, xs=xs, mp=mp: e.matmul(
                                        ph[m % 2][:, 0:512],
                                        wq1b[mp % 2][:, 256 * kc + 128 * mo:
                                                      256 * kc + 128 * mo + 128],
                                        xn[:, kc, xs:xs + 512],
                                        start=(kc == 0), stop=(kc == DK - 1),
                                        skip_group_check=True),
                                    waits=wl)
                            t_wq1[(u, mp)] = t_last
                            if m % 2 == 0:
                                h_eng[(u, m)] = "VE"
                                h_ev[(u, m)] = op("vector", "VE",
                                    lambda e, m=m: e.tensor_scalar(
                                        h_sb[:, m, :], ph[m % 2][:, 0:512],
                                        b1t[:, m:m + 1], 0.0, ALU.add, ALU.max),
                                    waits=[("TE", t_last)])
                            else:
                                h_eng[(u, m)] = "SE"
                                h_ev[(u, m)] = op("scalar", "SE",
                                    lambda e, m=m: e.activation(
                                        h_sb[:, m, :], ph[m % 2][:, 0:512], AF.Relu,
                                        bias=b1t[:, m:m + 1]),
                                    waits=[("TE", t_last)])
                    for dc in range(DK):
                        wv = []
                        if (u, dc - 2) in t_ffn2_last:
                            wv.append(("TE", t_ffn2_last[(u, dc - 2)]))
                        elif u > 0 and dc < 2:
                            wv.append(("TE", t_ffn2_last[(u - 1, 6 + dc)]))
                        if u == 0 and dc < 2:
                            wv.append(("cQW0", sq_scr0))
                            wv.append(("cQW1", sq_scr1))
                            wv += [("TE", h_end["TE"]), ("VE", h_end["VE"]),
                                   ("SE", h_end["SE"])]
                        d_w2t = dma(
                            f"cW2{dc % 2}",
                            w2tb[dc % 2][:],
                            bass.AP(wq2_sc.tensor, dc * (128 * 32 * 128),
                                    [[32 * 128, 128], [1, 32 * 128]]),
                            waits=wv, q="gpsimd")
                        t_last = None
                        for kc2 in range(MF):
                            wl = []
                            if kc2 == 0:
                                wl = [(f"cW2{dc % 2}", d_w2t),
                                      ("VE", h_ev[(u, 30)]), ("SE", h_ev[(u, 31)]),
                                      ("SE", s_gev.get((u, dc - 2)))]
                            t_last = op("tensor", "TE",
                                lambda e, dc=dc, kc2=kc2: e.matmul(
                                    pg[dc % 2][:, 0:512],
                                    w2tb[dc % 2][:, 128 * kc2:128 * kc2 + 128],
                                    h_sb[:, kc2, :],
                                    start=(kc2 == 0), stop=(kc2 == MF - 1),
                                    skip_group_check=True),
                                waits=wl)
                        t_ffn2_last[(u, dc)] = t_last
                        s_gev[(u, dc)] = op("scalar", "SE",
                            lambda e, dc=dc: e.activation(
                                g_fm[:, dc, :], pg[dc % 2][:, 0:512], AF.Silu,
                                bias=b2t[:, dc:dc + 1]),
                            waits=[("TE", t_last)])
                    for tt in range(4):
                        tc = 4 * u + tt
                        buf = tt % 2
                        t_tr = None
                        for dc2 in range(DK):
                            wl = []
                            if dc2 == 0:
                                wl = [("SE", s_gev[(u, DK - 1)])]
                                if tc >= 2:
                                    wl.append(("VE", v_gt.get(tc - 2)))
                            t_tr = op("tensor", "TE",
                                lambda e, buf=buf, dc2=dc2, tt=tt: e.transpose(
                                    pgt[buf][:, 128 * dc2:128 * (dc2 + 1)],
                                    g_fm[0:128, dc2, 128 * tt:128 * (tt + 1)],
                                    ident[:]),
                                waits=wl)
                        v_gt[tc] = op("vector", "VE",
                            lambda e, buf=buf, tc=tc: e.tensor_copy(
                                g_tm[:, tc, :], pgt[buf][:]),
                            waits=[("TE", t_tr)])

            PH["ffn"] = len(steps)
            with contextlib.ExitStack() as es_p2:
                psf = [es_p2.enter_context(nc.psum_tensor(f"psf{i}", [128, D], F32))
                       for i in range(2)]
                v_ob = {}
                d_out = {}
                for tc in range(N_TC):
                    buf = tc % 2
                    d_x = dma(f"cXT{buf}", xtf[buf][:],
                              x_tm_d[HALF + 128 * tc:HALF + 128 * tc + 128, :],
                              waits=[("VE", v_ob.get(tc - 2)),
                                     ("VE", v_wo_last if tc < 2 else None)])
                    t_last = None
                    for hh in range(2):
                        for dc in range(DK):
                            wl = []
                            if hh == 0 and dc == 0:
                                wl = [("VE", v_div_last), ("VE", v_wo_last),
                                      ("VE", v_gt[tc]), ("VE", v_consts)]
                                if tc == 0:
                                    wl.append(("VE", v_gt[N_TC - 1]))
                                    wl.append(("SE", s_gev[(3, DK - 1)]))
                                if tc >= 2:
                                    wl.append(("VE", v_ob.get(tc - 2)))
                            t_last = op("tensor", "TE",
                                lambda e, buf=buf, hh=hh, dc=dc, tc=tc: e.matmul(
                                    psf[buf][:, 512 * hh:512 * hh + 512],
                                    attn_r[:, dc, 128 * tc:128 * (tc + 1)],
                                    wout_b[:, dc, 512 * hh:512 * hh + 512],
                                    start=(dc == 0), stop=False,
                                    skip_group_check=True),
                                waits=wl)
                        t_last = op("tensor", "TE",
                            lambda e, buf=buf, hh=hh: e.matmul(
                                psf[buf][:, 512 * hh:512 * hh + 512],
                                ones_rb[:], bout_b[0:1, 512 * hh:512 * hh + 512],
                                start=False, stop=True, skip_group_check=True))
                    v_stt = op("vector", "VE",
                        lambda e, buf=buf, tc=tc: e.scalar_tensor_tensor(
                            ob[buf][:], psf[buf][:], 0.0, g_tm[:, tc, :],
                            ALU.add, ALU.mult),
                        waits=[("TE", t_last), (f"cXT{buf}", d_x),
                               (f"cOB{buf}", d_out.get(tc - 2))])
                    v_ob[tc] = op("vector", "VE", lambda e, buf=buf: e.tensor_tensor(
                        ob[buf][:], ob[buf][:], xtf[buf][:], ALU.add),
                        waits=[("VE", v_stt)])
                    d_out[tc] = dma(f"cOB{buf}", out_d[128 * tc:128 * (tc + 1), :],
                                    ob[buf][:], waits=[("VE", v_ob[tc])])

                fin0 = C.get("cOB0", 0)
                fin1 = C.get("cOB1", 0)

                def f_fin(e, fin0=fin0, fin1=fin1):
                    e.wait_ge(SEM["cOB0"], fin0)
                    e.wait_ge(SEM["cOB1"], fin1)
                steps.append(("sync", f_fin, 0))

        PH["all"] = len(steps)
        nc._phase_marks = PH
        nc._steps_meta = [(s[0], s[2]) for s in steps]
        block = es.enter_context(nc.Block())
        use_steps = steps if cut is None else steps[:cut]

        def mk(section_name):
            def _(eng):
                for st in use_steps:
                    if st[0] == section_name:
                        st[1](eng)
            return _

        block.sync(mk("sync"))
        block.tensor(mk("tensor"))
        block.vector(mk("vector"))
        block.scalar(mk("scalar"))
        block.gpsimd(mk("gpsimd"))

    return nc


_NC_CACHE = None


def _get_nc():
    global _NC_CACHE
    if _NC_CACHE is None:
        _NC_CACHE = build_nc()
    return _NC_CACHE


def _host_masks():
    m = np.zeros((128, 17 * MASKW), dtype=np.float32)
    for kc in range(17):
        j0, w = KCS[kc]
        tb, te = _bwin(kc)
        jj = np.arange(w)[:, None]
        tt = np.arange(te - tb)[None, :]
        m[0:w, MASKW * kc:MASKW * kc + (te - tb)] = (
            np.abs((tb + tt) - (j0 + jj)) <= HALF).astype(np.float32)
    return m.astype(ml_dtypes.bfloat16)


def kernel(L0_attn_x, L0_attn_w_qkv, L0_attn_proj, L0_attn_w_out, L0_attn_b_out,
           L0_ffn_W1, L0_ffn_b1, L0_ffn_W2, L0_ffn_b2):
    x = np.asarray(L0_attn_x, dtype=np.float32)
    wqkv = np.ascontiguousarray(np.asarray(L0_attn_w_qkv, dtype=np.float32))
    proj = np.ascontiguousarray(np.asarray(L0_attn_proj, dtype=np.float32))
    wout = np.ascontiguousarray(np.asarray(L0_attn_w_out, dtype=np.float32))
    bout = np.ascontiguousarray(np.asarray(L0_attn_b_out, dtype=np.float32).reshape(1, D))
    w1 = np.asarray(L0_ffn_W1, dtype=np.float32)
    b1 = np.asarray(L0_ffn_b1, dtype=np.float32)
    w2 = np.asarray(L0_ffn_W2, dtype=np.float32)
    b2 = np.asarray(L0_ffn_b2, dtype=np.float32)

    w1t = np.ascontiguousarray(w1.T)
    w2t = np.ascontiguousarray(w2.T)
    b1t = np.ascontiguousarray(b1.reshape(MF, 128).T)
    b2t = np.ascontiguousarray(b2.reshape(DK, 128).T)
    ident = np.eye(128, dtype=ml_dtypes.bfloat16)
    masks = _host_masks()

    in_maps = []
    for c in range(8):
        seq, t0 = c // 4, (c % 4) * T
        sl = np.zeros((P, D), dtype=np.float32)
        lo, hi = max(0, t0 - HALF), min(8192, t0 + T + HALF)
        sl[lo - (t0 - HALF):lo - (t0 - HALF) + (hi - lo)] = x[seq, lo:hi]
        em = np.ones((64, 6), dtype=np.float32)
        if t0 == 0:
            em[:, 0:3] = 0.0
        if t0 + T == 8192:
            em[:, 3:6] = 0.0
        in_maps.append({
            "x_tm": sl, "wqkv": wqkv, "proj": proj, "wout": wout, "bout": bout,
            "w1t": w1t, "b1t": b1t, "w2t": w2t, "b2t": b2t,
            "ident": ident, "masks": masks, "emask": em,
        })

    nc = _get_nc()
    res = run_bass_kernel_spmd(nc, in_maps, core_ids=list(range(8)))
    out = np.empty((2, 8192, D), dtype=np.float32)
    for c in range(8):
        seq, t0 = c // 4, (c % 4) * T
        out[seq, t0:t0 + T] = res.results[c]["out"]
    return out


# version 05
# speedup vs baseline: 1.4727x; 1.4727x over previous
import contextlib
import numpy as np
import ml_dtypes
import concourse.bass as bass
import concourse.mybir as mybir
from concourse.bass_utils import run_bass_kernel_spmd

F32 = mybir.dt.float32
BF16 = mybir.dt.bfloat16
AF = mybir.ActivationFunctionType
ALU = mybir.AluOpType

P = 2054
T = 2048
HALF = 3
D = 1024
DK = 8
NH = 8
FD = 64
DFF = 4096
MF = 32
MAGIC = 12582912.0
EPS_LN = 1e-5
EPS_Q = 1e-5
EPS_Z = 1e-8
N_ELEM_W = float(DFF * D)
MASKW = 134
N_XT = 17
N_TC = 16

SLABS5 = [(0, 512), (512, 512), (1024, 512), (1536, 512), (2048, 6)]
KCS = [(k * 128, min(128, P - k * 128)) for k in range(17)]


def _bwin(kc):
    j0, w = KCS[kc]
    return max(HALF, j0 - HALF), min(P - HALF, j0 + w + HALF)


SLAB_PARTS = [[] for _ in range(4)]
SLAB_LAST_KC = [0] * 4
for _kc in range(17):
    _tb, _te = _bwin(_kc)
    for _u in range(4):
        _su, _eu = HALF + 512 * _u, HALF + 512 * (_u + 1)
        _lo, _hi = max(_tb, _su), min(_te, _eu)
        if _lo < _hi:
            SLAB_PARTS[_u].append((_kc, _lo - _tb, _hi - _tb, _lo - _su))
            SLAB_LAST_KC[_u] = max(SLAB_LAST_KC[_u], _kc)


def build_nc(cut=None):
    nc = bass.Bass("TRN2", target_bir_lowering=False)

    x_tm_d = nc.dram_tensor("x_tm", [P, D], F32, kind="ExternalInput").ap()
    wqkv_d = nc.dram_tensor("wqkv", [D, 3 * D], F32, kind="ExternalInput").ap()
    proj_d = nc.dram_tensor("proj", [128, FD], F32, kind="ExternalInput").ap()
    wout_d = nc.dram_tensor("wout", [D, D], F32, kind="ExternalInput").ap()
    bout_d = nc.dram_tensor("bout", [1, D], F32, kind="ExternalInput").ap()
    w1t_d = nc.dram_tensor("w1t", [D, DFF], F32, kind="ExternalInput").ap()
    b1t_d = nc.dram_tensor("b1t", [128, MF], F32, kind="ExternalInput").ap()
    w2t_d = nc.dram_tensor("w2t", [DFF, D], F32, kind="ExternalInput").ap()
    b2t_d = nc.dram_tensor("b2t", [128, DK], F32, kind="ExternalInput").ap()
    ident_d = nc.dram_tensor("ident", [128, 128], BF16, kind="ExternalInput").ap()
    masks_d = nc.dram_tensor("masks", [128, 17 * MASKW], BF16, kind="ExternalInput").ap()
    emask_d = nc.dram_tensor("emask", [64, 6], F32, kind="ExternalInput").ap()
    out_d = nc.dram_tensor("out", [T, D], F32, kind="ExternalOutput").ap()

    wq1_sc = nc.dram_tensor("wq1_sc", [16, 128, 8, 256], BF16, kind="Internal").ap()
    wq2_sc = nc.dram_tensor("wq2_sc", [8, 128, 32, 128], BF16, kind="Internal").ap()
    zdram = nc.dram_tensor("zdram", [NH, T], BF16, kind="Internal").ap()
    zdram_f = nc.dram_tensor("zdram_f", [NH, T], F32, kind="Internal").ap()

    steps = []
    PH = {}
    C = {"SQ": 0, "TE": 0, "VE": 0, "SE": 0, "GE": 0}

    def bump(name, n=1):
        C[name] += n
        return C[name]

    es = contextlib.ExitStack()
    with es:
        sq = es.enter_context(nc.semaphore("sq"))
        te_s = es.enter_context(nc.semaphore("te"))
        ve_s = es.enter_context(nc.semaphore("ve"))
        se_s = es.enter_context(nc.semaphore("se"))
        ge_s = es.enter_context(nc.semaphore("ge"))
        SEM = {"SQ": sq, "TE": te_s, "VE": ve_s, "SE": se_s, "GE": ge_s}

        sem_es = es

        def dch(name):
            if name not in SEM:
                SEM[name] = sem_es.enter_context(nc.semaphore(name))
                C[name] = 0
            return SEM[name]

        def dma(ch, out, in_, waits=(), q="sync"):
            h = dch(ch)
            w = [(s, v) for (s, v) in waits if v]
            val = bump(ch, 16)

            def f(e, out=out, in_=in_, w=w, h=h):
                for (s, v) in w:
                    e.wait_ge(SEM[s], v)
                e.dma_start(out=out, in_=in_).then_inc(h, 16)
            import traceback
            steps.append((q, f, traceback.extract_stack()[-2].lineno))
            return val

        def op(eng, sem_name, emitf, waits=(), inc=1):
            w = [(s, v) for (s, v) in waits if v]
            val = bump(sem_name, inc) if inc else C[sem_name]

            def f(e, emitf=emitf, w=w, inc=inc, sem_name=sem_name):
                for (s, v) in w:
                    e.wait_ge(SEM[s], v)
                ins = emitf(e)
                if inc:
                    ins.then_inc(SEM[sem_name], inc)
            import traceback
            steps.append((eng, f, traceback.extract_stack()[-2].lineno))
            return val

        ident = es.enter_context(nc.sbuf_tensor("sb_ident", [128, 128], BF16))
        emask = es.enter_context(nc.sbuf_tensor("sb_emask", [64, 6], F32))
        proj_f = es.enter_context(nc.sbuf_tensor("proj_f", [128, FD], F32))
        proj_b = es.enter_context(nc.sbuf_tensor("proj_b", [128, FD], BF16))
        b1t = es.enter_context(nc.sbuf_tensor("sb_b1t", [128, MF], F32))
        b2t = es.enter_context(nc.sbuf_tensor("sb_b2t", [128, DK], F32))
        bout_f = es.enter_context(nc.sbuf_tensor("bout_f", [1, D], F32))
        bout_b = es.enter_context(nc.sbuf_tensor("bout_b", [1, D], BF16))
        ones_rb = es.enter_context(nc.sbuf_tensor("ones_rb", [1, 128], BF16))
        ones_mb = es.enter_context(nc.sbuf_tensor("ones_mb", [128, 128], BF16))
        ones_cf = es.enter_context(nc.sbuf_tensor("ones_cf", [128, 1], F32))
        magic_c = es.enter_context(nc.sbuf_tensor("magic_c", [128, 1], F32))
        ones_rf = es.enter_context(nc.sbuf_tensor("ones_rf", [1, 128], F32))

        xn = es.enter_context(nc.sbuf_tensor("xn", [128, DK, P], BF16))
        attn_r = es.enter_context(nc.sbuf_tensor("attn_r", [128, NH, T], BF16))

        v_c0 = 0
        for (dst, src) in [(ident[:], ident_d), (emask[:], emask_d),
                           (proj_f[:], proj_d), (b1t[:], b1t_d), (b2t[:], b2t_d),
                           (bout_f[:], bout_d)]:
            v_c0 = dma("cA", dst, src)
        op("vector", "VE", lambda e: e.memset(ones_rb[:], 1.0), inc=0)
        op("vector", "VE", lambda e: e.memset(ones_mb[:], 1.0), inc=0)
        op("vector", "VE", lambda e: e.memset(ones_cf[:], 1.0), inc=0)
        op("vector", "VE", lambda e: e.memset(magic_c[:], MAGIC), inc=0)
        op("vector", "VE", lambda e: e.memset(ones_rf[:], 1.0), inc=0)
        op("vector", "VE", lambda e: e.tensor_copy(proj_b[:], proj_f[:]),
           waits=[("cA", v_c0)], inc=0)
        v_consts = op("vector", "VE", lambda e: e.tensor_copy(bout_b[:], bout_f[:]),
                      waits=[("cA", v_c0)])

        v_vevac = {}
        v_aevac = {}
        v_zevac = {}
        v_div = {}

        with contextlib.ExitStack() as es_vh:
            v_tm = es_vh.enter_context(nc.sbuf_tensor("v_tm", [128, N_XT, D], BF16))

            tr_done = {}
            v_xn_evac = {}
            with contextlib.ExitStack() as es_lnv:
                ptr = [es_lnv.enter_context(nc.psum_tensor(f"ptr{i}", [128, D], BF16))
                       for i in range(2)]
                pv = [es_lnv.enter_context(nc.psum_tensor(f"pv{i}", [128, D], F32))
                      for i in range(2)]

                with contextlib.ExitStack() as es_ln:
                    xtm = es_ln.enter_context(nc.sbuf_tensor("xtm", [128, N_XT, D], F32))
                    junk = es_ln.enter_context(nc.sbuf_tensor("junk", [128, D], F32))
                    ssum = es_ln.enter_context(nc.sbuf_tensor("ssum", [128, N_XT], F32))
                    ssq = es_ln.enter_context(nc.sbuf_tensor("ssq", [128, N_XT], F32))
                    m2c = es_ln.enter_context(nc.sbuf_tensor("m2c", [128, N_XT], F32))
                    rstd = es_ln.enter_context(nc.sbuf_tensor("rstd", [128, N_XT], F32))
                    nmr = es_ln.enter_context(nc.sbuf_tensor("nmr", [128, N_XT], F32))
                    xnt = es_ln.enter_context(nc.sbuf_tensor("xnt", [128, 2, D], BF16))

                    dma("cX", xtm[:, 0:16, :],
                        bass.AP(x_tm_d.tensor, 0, [[D, 128], [128 * D, 16], [1, D]]))
                    v_x1 = dma("cX", xtm[0:6, 16, :], x_tm_d[2048:2054, :])

                    op("vector", "VE", lambda e: e.memset(ssum[:], 0.0), inc=0)
                    v_ms = op("vector", "VE", lambda e: e.memset(ssq[:], 0.0))
                    s_stats = None
                    for tc in range(N_XT):
                        rows = 128 if tc < 16 else 6
                        wv = [("cX", v_x1), ("VE", v_ms)] if tc == 0 else [("SE", s_stats)]
                        s_prev = op("scalar", "SE", lambda e, tc=tc, rows=rows: e.activation(
                            junk[0:rows, :], xtm[0:rows, tc, :], AF.Square,
                            accum_out=ssq[0:rows, tc:tc + 1]), waits=wv)
                        s_stats = op("scalar", "SE", lambda e, tc=tc, rows=rows: e.activation(
                            junk[0:rows, :], xtm[0:rows, tc, :], AF.Identity,
                            accum_out=ssum[0:rows, tc:tc + 1]),
                            waits=[("SE", s_prev)])

                    w1_ = op("vector", "VE", lambda e: e.tensor_scalar(
                        ssum[:], ssum[:], 1.0 / D, None, ALU.mult),
                        waits=[("SE", s_stats)])
                    w2_ = op("vector", "VE", lambda e: e.tensor_tensor(
                        m2c[:], ssum[:], ssum[:], ALU.mult), waits=[("VE", w1_)])
                    w3_ = op("vector", "VE", lambda e: e.scalar_tensor_tensor(
                        ssq[:], ssq[:], 1.0 / D, m2c[:], ALU.mult, ALU.subtract),
                        waits=[("VE", w2_)])
                    w4_ = op("vector", "VE", lambda e: e.tensor_scalar(
                        ssq[:], ssq[:], EPS_LN, None, ALU.add), waits=[("VE", w3_)])
                    v_stats = op("vector", "VE", lambda e: e.reciprocal(m2c[:], ssq[:]),
                                 waits=[("VE", w4_)])
                    s_rstd = op("scalar", "SE", lambda e: e.activation(
                        rstd[:], m2c[:], AF.Sqrt), waits=[("VE", v_stats)])
                    v_nmr = op("vector", "VE", lambda e: e.scalar_tensor_tensor(
                        nmr[:], ssum[:], -1.0, rstd[:], ALU.mult, ALU.mult),
                        waits=[("SE", s_rstd), ("VE", v_stats)])

                    for tc in range(N_XT):
                        rows = 128 if tc < 16 else 6
                        buf = tc % 2
                        s_w = [("VE", v_nmr)]
                        if tc >= 2:
                            s_w.append(("TE", tr_done[tc - 2]))
                        s_xn = op("scalar", "SE",
                            lambda e, tc=tc, rows=rows, buf=buf: e.activation(
                                xnt[0:rows, buf, :], xtm[0:rows, tc, :], AF.Identity,
                                scale=rstd[0:rows, tc:tc + 1],
                                bias=nmr[0:rows, tc:tc + 1]),
                            waits=s_w)
                        for j in range(DK):
                            w_list = []
                            if j == 0:
                                w_list = [("SE", s_xn)]
                                if tc == 0:
                                    w_list.append(("cA", v_c0))
                                if tc >= 2:
                                    w_list.append(("VE", v_xn_evac[tc - 2]))
                            tr_done[tc] = op("tensor", "TE",
                                lambda e, rows=rows, buf=buf, j=j: e.transpose(
                                    ptr[buf][:, 128 * j:128 * j + rows],
                                    xnt[0:rows, buf, 128 * j:128 * (j + 1)],
                                    ident[0:rows, 0:rows]),
                                waits=w_list)
                        v_xn_evac[tc] = op("vector", "VE",
                            lambda e, tc=tc, rows=rows, buf=buf: e.tensor_copy(
                                xn[:, :, 128 * tc:128 * tc + rows],
                                ptr[buf][:].rearrange("p (j r) -> p j r", j=DK)[:, :, 0:rows]),
                            waits=[("TE", tr_done[tc])])

                PH["ln"] = len(steps)
                ln_end = dict(C)
                with contextlib.ExitStack() as es_v:
                    wvf = [es_v.enter_context(nc.sbuf_tensor(f"wvf{i}", [128, D], F32))
                           for i in range(2)]
                    wv_b = es_v.enter_context(nc.sbuf_tensor("wv_b", [128, DK, D], BF16))

                    v_wvc = {}
                    for kc in range(DK):
                        ch = f"cWV{kc % 2}"
                        xw = [("VE", v_wvc.get(kc - 2))]
                        if kc < 2:
                            xw += [("SE", ln_end["SE"]), ("TE", ln_end["TE"]),
                                   ("VE", ln_end["VE"])]
                        d_wv = dma(ch, wvf[kc % 2][:],
                                   wqkv_d[128 * kc:128 * kc + 128, 2048:3072],
                                   waits=xw)
                        v_wvc[kc] = op("vector", "VE", lambda e, kc=kc: e.tensor_copy(
                            wv_b[:, kc, :], wvf[kc % 2][:]), waits=[(ch, d_wv)])

                    for tc in range(N_XT):
                        rows = 128 if tc < 16 else 6
                        buf = tc % 2
                        t_last = None
                        for kc in range(DK):
                            for hh in range(2):
                                wl = []
                                if kc == 0 and hh == 0:
                                    wl = [("VE", v_xn_evac[tc]), ("VE", v_wvc[DK - 1])]
                                    if tc >= 2:
                                        wl.append(("VE", v_vevac[tc - 2]))
                                t_last = op("tensor", "TE",
                                    lambda e, tc=tc, rows=rows, buf=buf, kc=kc, hh=hh:
                                    e.matmul(
                                        pv[buf][0:rows, 512 * hh:512 * hh + 512],
                                        xn[:, kc, 128 * tc:128 * tc + rows],
                                        wv_b[:, kc, 512 * hh:512 * hh + 512],
                                        start=(kc == 0), stop=(kc == DK - 1),
                                        skip_group_check=True),
                                    waits=wl)
                        v_vevac[tc] = op("vector", "VE",
                            lambda e, tc=tc, rows=rows, buf=buf: e.tensor_copy(
                                v_tm[0:rows, tc, :], pv[buf][0:rows, :]),
                            waits=[("TE", t_last)])

            PH["v"] = len(steps)
            v_end = dict(C)
            with contextlib.ExitStack() as es_h:
                masks = es_h.enter_context(
                    nc.sbuf_tensor("sb_masks", [128, 17 * MASKW], BF16))
                whf = [es_h.enter_context(nc.sbuf_tensor(f"whf{i}", [128, D], F32))
                       for i in range(2)]
                whb = [es_h.enter_context(nc.sbuf_tensor(f"whb{i}", [128, D], BF16))
                       for i in range(2)]
                qfm = [es_h.enter_context(nc.sbuf_tensor(f"qfm{i}", [128, 512], BF16))
                       for i in range(2)]
                e_scr = [es_h.enter_context(nc.sbuf_tensor(f"e_scr{i}", [64, 512], F32))
                         for i in range(2)]
                r_scr = [es_h.enter_context(nc.sbuf_tensor(f"r_scr{i}", [64, 512], F32))
                         for i in range(2)]
                q_phi = es_h.enter_context(nc.sbuf_tensor("q_phi", [64, P], BF16))
                k_phi = es_h.enter_context(nc.sbuf_tensor("k_phi", [64, P], BF16))
                s_bf = [es_h.enter_context(nc.sbuf_tensor(f"s_bf{i}", [128, MASKW], BF16))
                        for i in range(4)]
                z_all = es_h.enter_context(nc.sbuf_tensor("z_all", [NH, T], F32))
                z2 = es_h.enter_context(nc.sbuf_tensor("z2", [NH, T], F32))
                zinv_b = es_h.enter_context(nc.sbuf_tensor("zinv_b", [NH, T], BF16))
                zrow_b = es_h.enter_context(nc.sbuf_tensor("zrow_b", [1, T], BF16))
                z_stage = [es_h.enter_context(nc.sbuf_tensor(f"z_stage{i}", [1, 512], F32))
                           for i in range(2)]
                qbuf = [es_h.enter_context(nc.sbuf_tensor(f"qb{i}", [128, 2048], F32))
                        for i in range(3)]
                qo = [es_h.enter_context(nc.sbuf_tensor(f"qo{i}", [128, 2048], BF16))
                      for i in range(2)]
                gam = es_h.enter_context(nc.sbuf_tensor("gam", [128, 32], F32))
                gc2 = es_h.enter_context(nc.sbuf_tensor("gc2", [128, 2], F32))
                grow = es_h.enter_context(nc.sbuf_tensor("grow", [1, 2], F32))
                gtmp = es_h.enter_context(nc.sbuf_tensor("gtmp", [128, 2], F32))
                invg = es_h.enter_context(nc.sbuf_tensor("invg", [128, 2], F32))
                pq = [es_h.enter_context(nc.psum_tensor(f"pq{i}", [128, 512], F32))
                      for i in range(2)]
                pS = [es_h.enter_context(nc.psum_tensor(f"pS{i}", [128, 512], F32))
                      for i in range(2)]
                pA = [es_h.enter_context(nc.psum_tensor(f"pA{i}", [128, 512], F32))
                      for i in range(2)]
                pZ = [es_h.enter_context(nc.psum_tensor(f"pZ{i}", [128, 512], F32))
                      for i in range(2)]

                d_masks = dma("cM", masks[:], masks_d,
                              waits=[("TE", v_end["TE"]), ("VE", v_end["VE"])])

                def qpiece_in(i):
                    if i < 16:
                        kc, hh = i // 2, i % 2
                        return w1t_d[128 * kc:128 * kc + 128,
                                     2048 * hh:2048 * hh + 2048]
                    j = i - 16
                    return bass.AP(w2t_d.tensor, 256 * j * D,
                                   [[D, 128], [128 * D, 2], [1, D]])

                def qpiece_out(i, buf):
                    src = qo[buf][:]
                    if i < 16:
                        kc, hh = i // 2, i % 2
                        o = bass.AP(wq1_sc.tensor,
                                    (8 * hh) * (128 * 8 * 256) + kc * 256,
                                    [[8 * 256, 128], [128 * 8 * 256, 8], [1, 256]])
                        s = src.rearrange("p (mp c) -> p mp c", mp=8)
                        return [(o, s)]
                    j = i - 16
                    pairs = []
                    for rr in range(2):
                        o = bass.AP(wq2_sc.tensor, (2 * j + rr) * 128,
                                    [[32 * 128, 128], [128 * 32 * 128, 8], [1, 128]])
                        s = bass.AP(qo[buf], rr * 1024,
                                    [[2048, 128], [128, 8], [1, 128]])
                        pairs.append((o, s))
                    return pairs

                q_dma = {}
                q_abs = {}
                q_p1 = {}
                q_p2 = {}
                q_p3 = {}
                q_wr = {}

                def emit_gamma_piece(i):
                    b = i % 3
                    xw = [("SE", q_abs.get(i - 3))]
                    if i < 3:
                        xw += [("TE", v_end["TE"]), ("VE", v_end["VE"])]
                    q_dma[i] = dma(f"cQB{b}", qbuf[b][:], qpiece_in(i), waits=xw, q="gpsimd")
                    q_abs[i] = op("scalar", "SE", lambda e, b=b, i=i: e.activation(
                        qo[i % 2][:], qbuf[b][:], AF.Abs,
                        accum_out=gam[:, i:i + 1]),
                        waits=[(f"cQB{b}", q_dma[i]), ("SE", q_abs.get(i - 2))])

                def emit_invg():
                    op("vector", "VE", lambda e: e.tensor_reduce(
                        gc2[:, 0:1], gam[:, 0:16], mybir.AxisListType.X, ALU.add),
                        waits=[("SE", q_abs[31])], inc=0)
                    v_gc = op("vector", "VE", lambda e: e.tensor_reduce(
                        gc2[:, 1:2], gam[:, 16:32], mybir.AxisListType.X, ALU.add))
                    t1 = op("tensor", "TE", lambda e: e.matmul(
                        pq[0][0:1, 0:2], ones_cf[:], gc2[:], start=True, stop=True,
                        skip_group_check=True), waits=[("VE", v_gc)])
                    v_gr = op("vector", "VE", lambda e: e.tensor_copy(
                        grow[:], pq[0][0:1, 0:2]), waits=[("TE", t1)])
                    t2 = op("tensor", "TE", lambda e: e.matmul(
                        pq[0][0:128, 2:4], ones_rf[:], grow[:], start=True, stop=True,
                        skip_group_check=True), waits=[("VE", v_gr)])
                    vg1 = op("vector", "VE", lambda e: e.tensor_scalar(
                        gtmp[:], pq[0][0:128, 2:4], 1.0 / N_ELEM_W, EPS_Q,
                        ALU.mult, ALU.add),
                        waits=[("TE", t2)])
                    return op("vector", "VE", lambda e: e.reciprocal(invg[:], gtmp[:]),
                              waits=[("VE", vg1)])

                def emit_quant_piece(i, v_invg):
                    b = i % 3
                    bo = i % 2
                    xw = [("VE", q_p3.get(i - 3))]
                    if i < 3:
                        xw.append(("SE", q_abs[{0: 30, 1: 31, 2: 29}[i]]))
                    q_dma[100 + i] = dma(f"cQB{b}", qbuf[b][:], qpiece_in(i),
                                         waits=xw, q="gpsimd")
                    col = 0 if i < 16 else 1
                    q_p1[i] = op("scalar", "SE", lambda e, b=b, col=col: e.activation(
                        qbuf[b][:], qbuf[b][:], AF.Identity,
                        scale=invg[:, col:col + 1], bias=magic_c[:]),
                        waits=[(f"cQB{b}", q_dma[100 + i]), ("VE", v_invg),
                               ("SE", q_abs[{0: 30, 1: 31, 2: 29}.get(i, 0)]
                                if i < 3 else None)])
                    q_p2[i] = op("vector", "VE", lambda e, b=b: e.tensor_scalar(
                        qbuf[b][:], qbuf[b][:], MAGIC, 1.0, ALU.subtract, ALU.min),
                        waits=[("SE", q_p1[i])])
                    q_p3[i] = op("vector", "VE", lambda e, b=b, bo=bo: e.tensor_scalar(
                        qo[bo][:], qbuf[b][:], -1.0, None, ALU.max),
                        waits=[(f"cQW{bo}", q_wr.get(i - 2)), ("VE", q_p2[i])])
                    for (o, s) in qpiece_out(i, bo):
                        q_wr[i] = dma(f"cQW{bo}", o, s, waits=[("VE", q_p3[i])], q="gpsimd")

                d_zs = {}
                v_wcast = {}
                v_fmevac = {}
                v_phislab = {}
                s_exp = {}
                s_relu = {}
                v_sbf = {}
                v_phidone = {}
                t_sbf_free = {}
                v_invg_val = None

                for h in range(NH):
                    if h == 4:
                        v_invg_val = emit_invg()
                    pend = list(range(8 * (h % 4), 8 * (h % 4) + 8))

                    def emit_piece():
                        if not pend:
                            return
                        i = pend.pop(0)
                        if h < 4:
                            emit_gamma_piece(i)
                        else:
                            emit_quant_piece(i, v_invg_val)

                    for side in range(2):
                        coff = side * D + 128 * h
                        ch = f"cWH{side}"
                        xw = [("VE", v_wcast.get((h - 1, side)))]
                        if h == 0:
                            xw += [("TE", v_end["TE"]), ("VE", v_end["VE"])]
                        d_w = dma(ch, whf[side][:].rearrange("p (k c) -> p k c", k=8),
                                  bass.AP(wqkv_d.tensor, coff,
                                          [[3 * D, 128], [128 * 3 * D, 8], [1, 128]]),
                                  waits=xw)
                        v_wcast[(h, side)] = op("vector", "VE",
                            lambda e, side=side: e.tensor_copy(
                                whb[side][:], whf[side][:]),
                            waits=[(ch, d_w)])

                    for side, phi in enumerate([q_phi, k_phi]):
                        for si, (sb0, sw) in enumerate(SLABS5):
                            buf = si % 2
                            t_last = None
                            for kc in range(DK):
                                wl = []
                                if kc == 0:
                                    wl = [("VE", v_wcast[(h, side)]),
                                          ("VE", v_fmevac.get((h, side, si - 2)))]
                                    if side == 0 and si < 2:
                                        wl.append(("VE", v_sbf.get((h - 1, 14 + si))))
                                    if h == 4 and side == 0 and si == 0:
                                        wl.append(("VE", v_invg_val))
                                t_last = op("tensor", "TE",
                                    lambda e, buf=buf, kc=kc, sb0=sb0, sw=sw,
                                           side=side: e.matmul(
                                        pq[buf][:, 0:sw],
                                        whb[side][:, 128 * kc:128 * kc + 128],
                                        xn[:, kc, sb0:sb0 + sw],
                                        start=(kc == 0), stop=(kc == DK - 1),
                                        skip_group_check=True),
                                    waits=wl)
                            v_fmevac[(h, side, si)] = op("vector", "VE",
                                lambda e, buf=buf, sw=sw: e.tensor_copy(
                                    qfm[buf][:, 0:sw], pq[buf][:, 0:sw]),
                                waits=[("TE", t_last)])
                            wl = [("VE", v_fmevac[(h, side, si)]),
                                  ("VE", v_phislab.get((h, side, si - 2))),
                                  ("SE", s_relu.get((h, side, si - 2)))]
                            t_phi = op("tensor", "TE",
                                lambda e, buf=buf, sw=sw: e.matmul(
                                    pS[buf][0:64, 0:sw], proj_b[:],
                                    qfm[buf][:, 0:sw], start=True, stop=True,
                                    skip_group_check=True),
                                waits=wl)
                            s_exp[(h, side, si)] = op("scalar", "SE",
                                lambda e, buf=buf, sw=sw: e.activation(
                                    e_scr[buf][:, 0:sw], pS[buf][0:64, 0:sw], AF.Exp),
                                waits=[("TE", t_phi)])
                            s_relu[(h, side, si)] = op("scalar", "SE",
                                lambda e, buf=buf, sw=sw: e.activation(
                                    r_scr[buf][:, 0:sw], pS[buf][0:64, 0:sw], AF.Relu),
                                waits=[("SE", s_exp[(h, side, si)])])
                            v_phislab[(h, side, si)] = op("vector", "VE",
                                lambda e, buf=buf, sb0=sb0, sw=sw, phi=phi:
                                e.scalar_tensor_tensor(
                                    phi[:, sb0:sb0 + sw], e_scr[buf][:, 0:sw], 1.0,
                                    r_scr[buf][:, 0:sw], ALU.min, ALU.add),
                                waits=[("SE", s_relu[(h, side, si)]), ("VE",
                                       v_phislab.get((h, side, si - 2)))])
                            emit_piece()
                        v_phidone[(h, side)] = v_phislab[(h, side, 4)]

                    op("vector", "VE", lambda e: e.tensor_tensor(
                        k_phi[:, 0:3], k_phi[:, 0:3], emask[:, 0:3], ALU.mult),
                        waits=[("VE", v_phidone[(h, 1)])], inc=0)
                    v_kedge = op("vector", "VE", lambda e: e.tensor_tensor(
                        k_phi[:, 2051:2054], k_phi[:, 2051:2054], emask[:, 3:6],
                        ALU.mult))

                    first_a = [True] * 4
                    first_z = [True] * 4
                    watermark = [0] * 4
                    t_band = {}
                    pSp = [pS[0], pS[1], pq[0], pq[1]]
                    for kc in range(17):
                        j0, w = KCS[kc]
                        tb, te = _bwin(kc)
                        tw = te - tb
                        buf = kc % 4
                        wl = [("VE", v_sbf.get((h, kc - 4)))]
                        if kc == 0:
                            wl = [("VE", v_kedge), ("cM", d_masks),
                                  ("SE", s_relu[(h, 1, 4)])]
                        t_s = op("tensor", "TE",
                            lambda e, buf=buf, j0=j0, w=w, tb=tb, tw=tw: e.matmul(
                                pSp[buf][0:w, 0:tw], k_phi[:, j0:j0 + w],
                                q_phi[:, tb:tb + tw],
                                start=True, stop=True, skip_group_check=True),
                            waits=wl)
                        sbf_prev = t_sbf_free.get((h, kc - 4)) if kc >= 4 else \
                            t_sbf_free.get((h - 1, 13 + kc))
                        v_sbf[(h, kc)] = op("vector", "VE",
                            lambda e, buf=buf, w=w, tw=tw, kc=kc: e.tensor_tensor(
                                s_bf[buf][0:w, 0:tw], pSp[buf][0:w, 0:tw],
                                masks[0:w, MASKW * kc:MASKW * kc + tw], ALU.mult),
                            waits=[("TE", t_s), ("TE", sbf_prev)])
                        for u in range(4):
                            for (k2, a, b_, pa) in SLAB_PARTS[u]:
                                if k2 != kc:
                                    continue
                                wl_a = [("VE", v_sbf[(h, kc)])]
                                if h == 0 and first_a[u]:
                                    wl_a.append(("VE", v_vevac[N_XT - 1]))
                                if first_a[u]:
                                    pkey = (h, u - 2) if u >= 2 else (h - 1, u + 2)
                                    wl_a.append(("VE", v_aevac.get(pkey)))
                                    wl_a.append(("VE", v_zevac.get(pkey)))
                                wm = watermark[u]
                                pieces = []
                                if pa < wm:
                                    pieces.append((a, a + min(wm - pa, b_ - a), pa, False))
                                if pa + (b_ - a) > wm:
                                    a2 = a + max(0, wm - pa)
                                    pieces.append((a2, b_, pa + (a2 - a),
                                                   first_a[u]))
                                for (aa, bb, pp, st_flag) in pieces:
                                    op("tensor", "TE",
                                        lambda e, u=u, aa=aa, bb=bb, pp=pp, buf=buf,
                                               w=w, kc=kc, h=h, fa=st_flag: e.matmul(
                                            pA[u % 2][:, pp:pp + (bb - aa)],
                                            v_tm[0:w, kc, 128 * h:128 * h + 128],
                                            s_bf[buf][0:w, aa:bb],
                                            start=fa, stop=True,
                                            skip_group_check=True),
                                        waits=wl_a)
                                    wl_a = []
                                    t_z = op("tensor", "TE",
                                        lambda e, u=u, aa=aa, bb=bb, pp=pp, buf=buf,
                                               w=w, fz=st_flag: e.matmul(
                                            pZ[u % 2][:, pp:pp + (bb - aa)],
                                            ones_mb[0:w, :], s_bf[buf][0:w, aa:bb],
                                            start=fz, stop=True,
                                            skip_group_check=True))
                                watermark[u] = max(wm, pa + (b_ - a))
                                first_a[u] = False
                                first_z[u] = False
                                t_band[u] = t_z
                                t_sbf_free[(h, kc)] = t_z
                        for u in range(4):
                            if SLAB_LAST_KC[u] != kc:
                                continue
                            idx = h * 4 + u
                            v_aevac[(h, u)] = op("vector", "VE",
                                lambda e, u=u, h=h: e.tensor_copy(
                                    attn_r[:, h, 512 * u:512 * u + 512],
                                    pA[u % 2][:, 0:512]),
                                waits=[("TE", t_band[u])])
                            v_zevac[(h, u)] = op("vector", "VE",
                                lambda e, u=u: e.tensor_copy(
                                    z_stage[u % 2][0:1, :], pZ[u % 2][0:1, 0:512]),
                                waits=[(f"cZS{u % 2}", d_zs.get(idx - 2))])
                            d_zs[idx] = dma(f"cZS{u % 2}",
                                            zdram_f[h:h + 1, 512 * u:512 * u + 512],
                                            z_stage[u % 2][0:1, :],
                                            waits=[("VE", v_zevac[(h, u)])])

                d_zall = dma("cZA", z_all[:], zdram_f,
                             waits=[("cZS0", d_zs[NH * 4 - 2]), ("cZS1", d_zs[NH * 4 - 1])])
                vz1 = op("vector", "VE", lambda e: e.tensor_scalar(
                    z_all[:], z_all[:], EPS_Z, None, ALU.add),
                    waits=[("cZA", d_zall)])
                vz2 = op("vector", "VE", lambda e: e.reciprocal(z2[:], z_all[:]),
                         waits=[("VE", vz1)])
                v_zi = op("vector", "VE", lambda e: e.tensor_copy(zinv_b[:], z2[:]),
                          waits=[("VE", vz2)])
                d_z1 = dma("cZI", zdram, zinv_b[:], waits=[("VE", v_zi)])
                for h in range(NH):
                    d_z2 = dma("cZR", zrow_b[0:1, :],
                               bass.AP(zdram.tensor, h * T, [[T, 1], [1, T]]),
                               waits=[("cZI", d_z1),
                                      ("VE", v_div.get(4 * h - 1))])
                    for u in range(4):
                        idx = h * 4 + u
                        buf = idx % 2
                        wl = [("cZR", d_z2)]
                        if idx >= 2:
                            wl.append(("VE", v_div[idx - 2]))
                        t_r = op("tensor", "TE", lambda e, buf=buf, u=u: e.matmul(
                            pS[buf][:, 0:512], ones_rb[:],
                            zrow_b[0:1, 512 * u:512 * u + 512],
                            start=True, stop=True, skip_group_check=True),
                            waits=wl)
                        v_div[idx] = op("vector", "VE",
                            lambda e, buf=buf, h=h, u=u: e.tensor_tensor(
                                attn_r[:, h, 512 * u:512 * u + 512],
                                attn_r[:, h, 512 * u:512 * u + 512],
                                pS[buf][:, 0:512], ALU.mult),
                            waits=[("TE", t_r)])
                v_div_last = v_div[NH * 4 - 1]
                sq_scr0 = C.get("cQW0", 0)
                sq_scr1 = C.get("cQW1", 0)
            h_end = dict(C)
            PH["heads"] = len(steps)

        with contextlib.ExitStack() as es_f:
            h_sb = es_f.enter_context(nc.sbuf_tensor("h_sb", [128, MF, 512], BF16))
            g_fm = es_f.enter_context(nc.sbuf_tensor("g_fm", [128, DK, 512], BF16))
            g_tm = es_f.enter_context(nc.sbuf_tensor("g_tm", [128, N_TC, D], BF16))
            wq1b = [es_f.enter_context(nc.sbuf_tensor(f"wq1b{i}", [128, 2048], BF16))
                    for i in range(2)]
            w2tb = [es_f.enter_context(nc.sbuf_tensor(f"w2tb{i}", [128, 32 * 128], BF16))
                    for i in range(2)]
            wout_b = es_f.enter_context(nc.sbuf_tensor("wout_b", [128, DK, D], BF16))
            xtf = [es_f.enter_context(nc.sbuf_tensor(f"xtf{i}", [128, D], F32))
                   for i in range(2)]
            ob = [es_f.enter_context(nc.sbuf_tensor(f"ob{i}", [128, D], F32))
                  for i in range(2)]

            v_woc = {}
            for dc in range(DK):
                ch = f"cXT{dc % 2}"
                xw = [("VE", v_woc.get(dc - 2))]
                if dc < 2:
                    xw += [("TE", h_end["TE"]), ("VE", h_end["VE"]),
                           ("SE", h_end["SE"]), ("cQW0", sq_scr0),
                           ("cQW1", sq_scr1)]
                d_wo = dma(ch, xtf[dc % 2][:], wout_d[128 * dc:128 * dc + 128, :],
                           waits=xw)
                v_woc[dc] = op("vector", "VE", lambda e, dc=dc: e.tensor_copy(
                    wout_b[:, dc, :], xtf[dc % 2][:]), waits=[(ch, d_wo)])
            v_wo_last = v_woc[DK - 1]

            s_gev = {}
            v_gt = {}
            with contextlib.ExitStack() as es_p1:
                ph = [es_p1.enter_context(nc.psum_tensor(f"ph{i}", [128, 512], F32))
                      for i in range(2)]
                pg = [es_p1.enter_context(nc.psum_tensor(f"pg{i}", [128, 512], F32))
                      for i in range(2)]
                pgt = [es_p1.enter_context(nc.psum_tensor(f"pgt{i}", [128, D], BF16))
                       for i in range(2)]

                h_ev = {}
                h_eng = {}
                t_wq1 = {}
                t_ffn2_last = {}
                for u in range(4):
                    xs = HALF + 512 * u
                    for mp in range(16):
                        wv = []
                        if (u, mp - 2) in t_wq1:
                            wv.append(("TE", t_wq1[(u, mp - 2)]))
                        elif u > 0 and mp < 2:
                            wv.append(("TE", t_wq1[(u - 1, 14 + mp)]))
                        if u == 0 and mp < 2:
                            wv.append(("cQW0", sq_scr0))
                            wv.append(("cQW1", sq_scr1))
                            wv += [("TE", h_end["TE"]), ("VE", h_end["VE"]),
                                   ("SE", h_end["SE"])]
                        d_wq1 = dma(
                            f"cW1{mp % 2}",
                            wq1b[mp % 2][:],
                            bass.AP(wq1_sc.tensor, mp * (128 * 8 * 256),
                                    [[8 * 256, 128], [1, 8 * 256]]),
                            waits=wv, q="gpsimd")
                        for mo in range(2):
                            m = 2 * mp + mo
                            t_last = None
                            for kc in range(DK):
                                wl = []
                                if kc == 0:
                                    wl = [(f"cW1{mp % 2}", d_wq1)]
                                    if u == 0 and m == 0:
                                        wl.append(("VE", v_div_last))
                                    if m >= 2:
                                        wl.append((h_eng[(u, m - 2)], h_ev[(u, m - 2)]))
                                    elif u > 0:
                                        wl.append((h_eng[(u - 1, 30 + m)],
                                                   h_ev[(u - 1, 30 + m)]))
                                t_last = op("tensor", "TE",
                                    lambda e, m=m, mo=mo, kc=kc, xs=xs, mp=mp: e.matmul(
                                        ph[m % 2][:, 0:512],
                                        wq1b[mp % 2][:, 256 * kc + 128 * mo:
                                                      256 * kc + 128 * mo + 128],
                                        xn[:, kc, xs:xs + 512],
                                        start=(kc == 0), stop=(kc == DK - 1),
                                        skip_group_check=True),
                                    waits=wl)
                            t_wq1[(u, mp)] = t_last
                            if m % 2 == 0:
                                h_eng[(u, m)] = "VE"
                                h_ev[(u, m)] = op("vector", "VE",
                                    lambda e, m=m: e.tensor_scalar(
                                        h_sb[:, m, :], ph[m % 2][:, 0:512],
                                        b1t[:, m:m + 1], 0.0, ALU.add, ALU.max),
                                    waits=[("TE", t_last)])
                            else:
                                h_eng[(u, m)] = "SE"
                                h_ev[(u, m)] = op("scalar", "SE",
                                    lambda e, m=m: e.activation(
                                        h_sb[:, m, :], ph[m % 2][:, 0:512], AF.Relu,
                                        bias=b1t[:, m:m + 1]),
                                    waits=[("TE", t_last)])
                    for dc in range(DK):
                        wv = []
                        if (u, dc - 2) in t_ffn2_last:
                            wv.append(("TE", t_ffn2_last[(u, dc - 2)]))
                        elif u > 0 and dc < 2:
                            wv.append(("TE", t_ffn2_last[(u - 1, 6 + dc)]))
                        if u == 0 and dc < 2:
                            wv.append(("cQW0", sq_scr0))
                            wv.append(("cQW1", sq_scr1))
                            wv += [("TE", h_end["TE"]), ("VE", h_end["VE"]),
                                   ("SE", h_end["SE"])]
                        d_w2t = dma(
                            f"cW2{dc % 2}",
                            w2tb[dc % 2][:],
                            bass.AP(wq2_sc.tensor, dc * (128 * 32 * 128),
                                    [[32 * 128, 128], [1, 32 * 128]]),
                            waits=wv, q="gpsimd")
                        t_last = None
                        for kc2 in range(MF):
                            wl = []
                            if kc2 == 0:
                                wl = [(f"cW2{dc % 2}", d_w2t),
                                      ("VE", h_ev[(u, 30)]), ("SE", h_ev[(u, 31)]),
                                      ("SE", s_gev.get((u, dc - 2)))]
                            t_last = op("tensor", "TE",
                                lambda e, dc=dc, kc2=kc2: e.matmul(
                                    pg[dc % 2][:, 0:512],
                                    w2tb[dc % 2][:, 128 * kc2:128 * kc2 + 128],
                                    h_sb[:, kc2, :],
                                    start=(kc2 == 0), stop=(kc2 == MF - 1),
                                    skip_group_check=True),
                                waits=wl)
                        t_ffn2_last[(u, dc)] = t_last
                        s_gev[(u, dc)] = op("scalar", "SE",
                            lambda e, dc=dc: e.activation(
                                g_fm[:, dc, :], pg[dc % 2][:, 0:512], AF.Silu,
                                bias=b2t[:, dc:dc + 1]),
                            waits=[("TE", t_last)])
                    for tt in range(4):
                        tc = 4 * u + tt
                        buf = tt % 2
                        t_tr = None
                        for dc2 in range(DK):
                            wl = []
                            if dc2 == 0:
                                wl = [("SE", s_gev[(u, DK - 1)])]
                                if tc >= 2:
                                    wl.append(("VE", v_gt.get(tc - 2)))
                            t_tr = op("tensor", "TE",
                                lambda e, buf=buf, dc2=dc2, tt=tt: e.transpose(
                                    pgt[buf][:, 128 * dc2:128 * (dc2 + 1)],
                                    g_fm[0:128, dc2, 128 * tt:128 * (tt + 1)],
                                    ident[:]),
                                waits=wl)
                        v_gt[tc] = op("vector", "VE",
                            lambda e, buf=buf, tc=tc: e.tensor_copy(
                                g_tm[:, tc, :], pgt[buf][:]),
                            waits=[("TE", t_tr)])

            PH["ffn"] = len(steps)
            with contextlib.ExitStack() as es_p2:
                psf = [es_p2.enter_context(nc.psum_tensor(f"psf{i}", [128, D], F32))
                       for i in range(2)]
                v_ob = {}
                d_out = {}
                for tc in range(N_TC):
                    buf = tc % 2
                    d_x = dma(f"cXT{buf}", xtf[buf][:],
                              x_tm_d[HALF + 128 * tc:HALF + 128 * tc + 128, :],
                              waits=[("VE", v_ob.get(tc - 2)),
                                     ("VE", v_wo_last if tc < 2 else None)])
                    t_last = None
                    for hh in range(2):
                        for dc in range(DK):
                            wl = []
                            if hh == 0 and dc == 0:
                                wl = [("VE", v_div_last), ("VE", v_wo_last),
                                      ("VE", v_gt[tc]), ("VE", v_consts)]
                                if tc == 0:
                                    wl.append(("VE", v_gt[N_TC - 1]))
                                    wl.append(("SE", s_gev[(3, DK - 1)]))
                                if tc >= 2:
                                    wl.append(("VE", v_ob.get(tc - 2)))
                            t_last = op("tensor", "TE",
                                lambda e, buf=buf, hh=hh, dc=dc, tc=tc: e.matmul(
                                    psf[buf][:, 512 * hh:512 * hh + 512],
                                    attn_r[:, dc, 128 * tc:128 * (tc + 1)],
                                    wout_b[:, dc, 512 * hh:512 * hh + 512],
                                    start=(dc == 0), stop=False,
                                    skip_group_check=True),
                                waits=wl)
                        t_last = op("tensor", "TE",
                            lambda e, buf=buf, hh=hh: e.matmul(
                                psf[buf][:, 512 * hh:512 * hh + 512],
                                ones_rb[:], bout_b[0:1, 512 * hh:512 * hh + 512],
                                start=False, stop=True, skip_group_check=True))
                    v_stt = op("vector", "VE",
                        lambda e, buf=buf, tc=tc: e.scalar_tensor_tensor(
                            ob[buf][:], psf[buf][:], 0.0, g_tm[:, tc, :],
                            ALU.add, ALU.mult),
                        waits=[("TE", t_last), (f"cXT{buf}", d_x),
                               (f"cOB{buf}", d_out.get(tc - 2))])
                    v_ob[tc] = op("vector", "VE", lambda e, buf=buf: e.tensor_tensor(
                        ob[buf][:], ob[buf][:], xtf[buf][:], ALU.add),
                        waits=[("VE", v_stt)])
                    d_out[tc] = dma(f"cOB{buf}", out_d[128 * tc:128 * (tc + 1), :],
                                    ob[buf][:], waits=[("VE", v_ob[tc])])

                fin0 = C.get("cOB0", 0)
                fin1 = C.get("cOB1", 0)

                def f_fin(e, fin0=fin0, fin1=fin1):
                    e.wait_ge(SEM["cOB0"], fin0)
                    e.wait_ge(SEM["cOB1"], fin1)
                steps.append(("sync", f_fin, 0))

        PH["all"] = len(steps)
        nc._phase_marks = PH
        nc._steps_meta = [(s[0], s[2]) for s in steps]
        block = es.enter_context(nc.Block())
        use_steps = steps if cut is None else steps[:cut]

        def mk(section_name):
            def _(eng):
                for st in use_steps:
                    if st[0] == section_name:
                        st[1](eng)
            return _

        block.sync(mk("sync"))
        block.tensor(mk("tensor"))
        block.vector(mk("vector"))
        block.scalar(mk("scalar"))
        block.gpsimd(mk("gpsimd"))

    return nc


_NC_CACHE = None


def _get_nc():
    global _NC_CACHE
    if _NC_CACHE is None:
        _NC_CACHE = build_nc()
    return _NC_CACHE


def _host_masks():
    m = np.zeros((128, 17 * MASKW), dtype=np.float32)
    for kc in range(17):
        j0, w = KCS[kc]
        tb, te = _bwin(kc)
        jj = np.arange(w)[:, None]
        tt = np.arange(te - tb)[None, :]
        m[0:w, MASKW * kc:MASKW * kc + (te - tb)] = (
            np.abs((tb + tt) - (j0 + jj)) <= HALF).astype(np.float32)
    return m.astype(ml_dtypes.bfloat16)


def kernel(L0_attn_x, L0_attn_w_qkv, L0_attn_proj, L0_attn_w_out, L0_attn_b_out,
           L0_ffn_W1, L0_ffn_b1, L0_ffn_W2, L0_ffn_b2):
    x = np.asarray(L0_attn_x, dtype=np.float32)
    wqkv = np.ascontiguousarray(np.asarray(L0_attn_w_qkv, dtype=np.float32))
    proj = np.ascontiguousarray(np.asarray(L0_attn_proj, dtype=np.float32))
    wout = np.ascontiguousarray(np.asarray(L0_attn_w_out, dtype=np.float32))
    bout = np.ascontiguousarray(np.asarray(L0_attn_b_out, dtype=np.float32).reshape(1, D))
    w1 = np.asarray(L0_ffn_W1, dtype=np.float32)
    b1 = np.asarray(L0_ffn_b1, dtype=np.float32)
    w2 = np.asarray(L0_ffn_W2, dtype=np.float32)
    b2 = np.asarray(L0_ffn_b2, dtype=np.float32)

    w1t = np.ascontiguousarray(w1.T)
    w2t = np.ascontiguousarray(w2.T)
    b1t = np.ascontiguousarray(b1.reshape(MF, 128).T)
    b2t = np.ascontiguousarray(b2.reshape(DK, 128).T)
    ident = np.eye(128, dtype=ml_dtypes.bfloat16)
    masks = _host_masks()

    in_maps = []
    for c in range(8):
        seq, t0 = c // 4, (c % 4) * T
        sl = np.zeros((P, D), dtype=np.float32)
        lo, hi = max(0, t0 - HALF), min(8192, t0 + T + HALF)
        sl[lo - (t0 - HALF):lo - (t0 - HALF) + (hi - lo)] = x[seq, lo:hi]
        em = np.ones((64, 6), dtype=np.float32)
        if t0 == 0:
            em[:, 0:3] = 0.0
        if t0 + T == 8192:
            em[:, 3:6] = 0.0
        in_maps.append({
            "x_tm": sl, "wqkv": wqkv, "proj": proj, "wout": wout, "bout": bout,
            "w1t": w1t, "b1t": b1t, "w2t": w2t, "b2t": b2t,
            "ident": ident, "masks": masks, "emask": em,
        })

    nc = _get_nc()
    res = run_bass_kernel_spmd(nc, in_maps, core_ids=list(range(8)))
    out = np.empty((2, 8192, D), dtype=np.float32)
    for c in range(8):
        seq, t0 = c // 4, (c % 4) * T
        out[seq, t0:t0 + T] = res.results[c]["out"]
    return out
